# version 05
# baseline (speedup 1.0000x reference)
import numpy as np
import concourse.bass as bass
import concourse.mybir as mybir
from concourse.bass_utils import run_bass_kernel_spmd

F32 = mybir.dt.float32
I32 = mybir.dt.int32
BF16 = mybir.dt.bfloat16
ALU = mybir.AluOpType
AX = mybir.AxisListType
ACTF = mybir.ActivationFunctionType

B, T, D, H, DH = 4, 4096, 2048, 16, 128
DFF = 8192
SCALE = float(1.0 / np.sqrt(np.float32(DH)))
NCORES = 8
OWN = T // 2
NTILE = OWN // 128



def emit_nr_recip(v, out, in_, tmp, iters=3):
    MAG = 0x7EF127EA >> 8
    v.tensor_scalar(tmp.bitcast(I32), in_.bitcast(I32), 8, None,
                    op0=ALU.logical_shift_right)
    v.drain()
    v.tensor_scalar(tmp.bitcast(I32), tmp.bitcast(I32), -1, MAG,
                    op0=ALU.mult, op1=ALU.add)
    v.drain()
    v.tensor_scalar(out.bitcast(I32), tmp.bitcast(I32), 8, None,
                    op0=ALU.logical_shift_left)
    v.drain()
    for _ in range(iters):
        v.tensor_tensor(tmp, in_, out, op=ALU.mult)
        v.drain()
        v.tensor_scalar(tmp, tmp, -1.0, 2.0, op0=ALU.mult, op1=ALU.add)
        v.drain()
        v.tensor_tensor(out, out, tmp, op=ALU.mult)
        v.drain()


def emit_quant(v, tiles, scr, r448):
    n = len(tiles)
    W = tiles[0][0].shape[-1]
    NB = W // 32
    guard = lambda: v.drain()
    m = [scr['m'][:, i * NB:(i + 1) * NB] for i in range(n)]
    s = [scr['s'][:, i * NB:(i + 1) * NB] for i in range(n)]
    rs = [scr['rs'][:, i * NB:(i + 1) * NB] for i in range(n)]
    i0 = [scr['i0'][:, i * W:(i + 1) * W] for i in range(n)]
    i1 = [scr['i1'][:, i * W:(i + 1) * W] for i in range(n)]
    f0 = [scr['f0'][:, i * W:(i + 1) * W] for i in range(n)]
    f1 = [scr['f1'][:, i * W:(i + 1) * W] for i in range(n)]

    def b3(ap):
        return ap.rearrange("p (b k) -> p b k", k=32)

    def bc(ap):
        return ap.unsqueeze(2).broadcast_to([128, NB, 32])

    for i, (x, _) in enumerate(tiles):
        v.tensor_reduce(m[i], b3(x), axis=AX.X, op=ALU.max,
                        apply_absolute_value=True)
    guard()
    for i in range(n):
        v.tensor_tensor(s[i], m[i], r448.broadcast_to([128, NB]), op=ALU.mult)
    guard()
    for i in range(n):
        v.tensor_scalar(s[i], s[i], 1e-30, None, op0=ALU.max)
    guard()
    MAG = 0x7EF127EA >> 8
    for i in range(n):
        v.tensor_scalar(rs[i].bitcast(I32), s[i].bitcast(I32), 8, None,
                        op0=ALU.logical_shift_right)
    guard()
    for i in range(n):
        v.tensor_scalar(rs[i].bitcast(I32), rs[i].bitcast(I32), -1, MAG,
                        op0=ALU.mult, op1=ALU.add)
    guard()
    for i in range(n):
        v.tensor_scalar(rs[i].bitcast(I32), rs[i].bitcast(I32), 8, None,
                        op0=ALU.logical_shift_left)
    guard()
    for _ in range(3):
        for i in range(n):
            v.tensor_tensor(m[i], s[i], rs[i], op=ALU.mult)
        guard()
        for i in range(n):
            v.tensor_scalar(m[i], m[i], -1.0, 2.0, op0=ALU.mult, op1=ALU.add)
        guard()
        for i in range(n):
            v.tensor_tensor(rs[i], rs[i], m[i], op=ALU.mult)
        guard()
    for i, (x, _) in enumerate(tiles):
        v.tensor_tensor(b3(i0[i].bitcast(F32)), b3(x), bc(rs[i]), op=ALU.mult)
    guard()
    for i in range(n):
        v.tensor_scalar(i0[i], i0[i], 0x7FFFFFFF, None, op0=ALU.bitwise_and)
    guard()
    for i in range(n):
        v.tensor_scalar(i1[i], i0[i], 0x7F800000, None, op0=ALU.bitwise_and)
    guard()
    for i in range(n):
        v.tensor_scalar(f0[i], i1[i].bitcast(F32), 0.125, 2.0 ** -9,
                        op0=ALU.mult, op1=ALU.max)
    guard()
    for i in range(n):
        v.scalar_tensor_tensor(f1[i], f0[i], 0.5, i0[i].bitcast(F32),
                               op0=ALU.mult, op1=ALU.add)
    guard()
    for i in range(n):
        v.tensor_scalar(i1[i], f0[i].bitcast(I32), -1, 254 << 23,
                        op0=ALU.mult, op1=ALU.add)
    guard()
    for i in range(n):
        v.tensor_tensor(i0[i].bitcast(F32), f1[i], i1[i].bitcast(F32),
                        op=ALU.mult)
    guard()
    for Tth in (8.0, 4.0, 2.0, 1.0, 1.0):
        for i in range(n):
            v.tensor_scalar(i1[i].bitcast(F32), i0[i].bitcast(F32), Tth, -Tth,
                            op0=ALU.is_ge, op1=ALU.mult)
        guard()
        for i in range(n):
            v.tensor_tensor(i0[i].bitcast(F32), i0[i].bitcast(F32),
                            i1[i].bitcast(F32), op=ALU.add)
        guard()
    for i in range(n):
        v.tensor_tensor(f0[i], i0[i].bitcast(F32), f0[i], op=ALU.mult)
    guard()
    for i in range(n):
        v.tensor_tensor(f1[i], f1[i], f0[i], op=ALU.subtract)
    guard()
    for i, (x, _) in enumerate(tiles):
        v.tensor_scalar(i1[i], x.bitcast(I32), -2147483648, None,
                        op0=ALU.bitwise_and)
    guard()
    for i in range(n):
        v.tensor_tensor(i0[i], f1[i].bitcast(I32), i1[i], op=ALU.bitwise_or)
    guard()
    for i, (_, out) in enumerate(tiles):
        v.tensor_tensor(b3(out), b3(i0[i].bitcast(F32)), bc(s[i]), op=ALU.mult)
    guard()


def quant_scratch(nc, es, name, wmax, n):
    return {
        'm': es.enter_context(nc.sbuf_tensor(name + "_m", [128, n * wmax // 32], F32)),
        's': es.enter_context(nc.sbuf_tensor(name + "_s", [128, n * wmax // 32], F32)),
        'rs': es.enter_context(nc.sbuf_tensor(name + "_rs", [128, n * wmax // 32], F32)),
        'i0': es.enter_context(nc.sbuf_tensor(name + "_i0", [128, n * wmax], I32)),
        'i1': es.enter_context(nc.sbuf_tensor(name + "_i1", [128, n * wmax], I32)),
        'f0': es.enter_context(nc.sbuf_tensor(name + "_f0", [128, n * wmax], F32)),
        'f1': es.enter_context(nc.sbuf_tensor(name + "_f1", [128, n * wmax], F32)),
    }



ENGS = ("sync", "tensor", "vector", "scalar", "gpsimd")

LOAD_SEMS = ("sldA", "wA", "wB", "sB0", "sB1", "p3L", "p3W", "p3bL",
             "p4X", "p4W", "p4Q", "p4R")
STORE_SEMS = ("st0", "st1", "st2", "st3", "ost0", "ost1", "p3S",
              "ts0", "ts1", "ts2", "ts3", "p4S", "p4O")


class _Cnt:
    __slots__ = ("v",)

    def __init__(self):
        self.v = 0


class Sched:
    def __init__(self):
        self.ev = {e: [] for e in ENGS}
        self.sems = {}
        self.cnt = {}

    def add_sem(self, name, handle):
        self.sems[name] = handle
        self.cnt[name] = _Cnt()

    def emit(self, eng, fn):
        self.ev[eng].append(fn)

    def op(self, eng, name, *a, **k):
        def f(e, name=name, a=a, k=k):
            getattr(e, name)(*a, **k)
        self.ev[eng].append(f)

    def dma(self, eng, out, in_, sem):
        c = self.cnt[sem]
        c.v += 16
        h = self.sems[sem]

        def f(e, out=out, in_=in_, h=h):
            e.dma_start(out=out, in_=in_).then_inc(h, 16)
        self.ev[eng].append(f)
        return c.v

    def wait(self, eng, sem, val):
        if val <= 0:
            return
        h = self.sems[sem]

        def f(e, h=h, val=val):
            e.wait_ge(h, val)
        self.ev[eng].append(f)

    def dfence(self, eng, sem):
        self.wait(eng, sem, self.cnt[sem].v)

    def pub(self, eng, sem):
        c = self.cnt[sem]
        c.v += 1
        h = self.sems[sem]

        def f(e, h=h):
            e.drain()
            e.sem_inc(h, 1)
        self.ev[eng].append(f)
        return c.v

    def barrier(self):
        for sem in LOAD_SEMS:
            self.dfence("sync", sem)
        for sem in STORE_SEMS:
            self.dfence("gpsimd", sem)
        for e in ENGS:
            self.pub(e, "ph")
        v = self.cnt["ph"].v
        for e in ENGS:
            self.wait(e, "ph", v)


class EngProxy:

    def __init__(self, s, eng):
        self._s = s
        self._e = eng

    def __getattr__(self, name):
        def f(*a, **k):
            self._s.op(self._e, name, *a, **k)
        return f



_NC_CACHE = {}


def build():
    if "nc" in _NC_CACHE:
        return _NC_CACHE["nc"]
    import contextlib
    nc = bass.Bass("TRN2", target_bir_lowering=False)
    S = Sched()

    xT = nc.dram_tensor("xT", [D, T], F32, kind="ExternalInput").ap()
    xoT = nc.dram_tensor("xoT", [D, OWN], F32, kind="ExternalInput").ap()
    xo = nc.dram_tensor("xo", [OWN, D], F32, kind="ExternalInput").ap()
    wqkv = nc.dram_tensor("wqkv", [D, 3 * D], F32, kind="ExternalInput").ap()
    wo = nc.dram_tensor("wo", [D, D], F32, kind="ExternalInput").ap()
    gw = nc.dram_tensor("gw", [DFF, D], F32, kind="ExternalInput").ap()
    uw = nc.dram_tensor("uw", [DFF, D], F32, kind="ExternalInput").ap()
    dw = nc.dram_tensor("dw", [D, DFF], F32, kind="ExternalInput").ap()
    m01 = nc.dram_tensor("m01", [128, 256], F32, kind="ExternalInput").ap()
    ident_d = nc.dram_tensor("ident", [128, 128], F32, kind="ExternalInput").ap()
    consts = nc.dram_tensor("consts", [128, 4], F32, kind="ExternalInput").ap()

    KT = nc.dram_tensor("KT", [D, T], F32, kind="Internal").ap()
    QT = nc.dram_tensor("QTs", [D, OWN], F32, kind="Internal").ap()
    Vd = nc.dram_tensor("Vd", [H, T, DH], F32, kind="Internal").ap()
    aT = nc.dram_tensor("aT", [D, OWN], F32, kind="Internal").ap()
    x1d = nc.dram_tensor("x1d", [OWN, D], F32, kind="Internal").ap()
    x1qT = nc.dram_tensor("x1qT", [D, OWN], F32, kind="Internal").ap()
    gqT = nc.dram_tensor("gqT", [D, DFF], F32, kind="Internal").ap()
    uqT = nc.dram_tensor("uqT", [D, DFF], F32, kind="Internal").ap()
    dqT = nc.dram_tensor("dqT", [DFF, D], F32, kind="Internal").ap()
    out_d = nc.dram_tensor("out", [OWN, D], F32, kind="ExternalOutput").ap()

    es0 = contextlib.ExitStack()
    with es0:
        for s in LOAD_SEMS + STORE_SEMS + ("pe", "vv", "aa", "ph"):
            S.add_sem(s, es0.enter_context(nc.semaphore(s)))
        cons = es0.enter_context(nc.sbuf_tensor("cons", [128, 4], F32))
        r448 = es0.enter_context(nc.sbuf_tensor("r448", [128, 1], F32))
        rn = es0.enter_context(nc.sbuf_tensor("rn", [128, 1], F32))
        nrtmp = es0.enter_context(nc.sbuf_tensor("nrtmp", [128, 1], F32))
        maskt = es0.enter_context(nc.sbuf_tensor("maskt", [128, 256], F32))
        idt = es0.enter_context(nc.sbuf_tensor("idt", [128, 128], F32))
        stats = es0.enter_context(nc.sbuf_tensor("stats", [128, 64], F32))
        ps = [es0.enter_context(nc.psum_tensor(f"ps{i}", [128, 512], F32))
              for i in range(8)]

        V = EngProxy(S, "vector")
        A = EngProxy(S, "scalar")

        t0 = S.dma("sync", cons[:], consts, "sldA")
        t1 = S.dma("sync", maskt[:], m01, "sldA")
        t2 = S.dma("sync", idt[:], ident_d, "sldA")
        S.wait("vector", "sldA", t2)
        emit_nr_recip(V, r448[:], cons[:, 0:1], nrtmp[:])
        emit_nr_recip(V, rn[:], cons[:, 2:3], nrtmp[:])
        S.pub("vector", "vv")

        import os
        nph = int(os.environ.get("KPHASES", "5"))
        if nph >= 1:
            _build_p1(nc, S, es0, xT, xoT, wqkv, KT, QT, Vd, ps)
            S.barrier()
        if nph >= 2:
            _build_p2(nc, S, es0, KT, QT, Vd, aT, maskt, idt, stats, ps)
            S.barrier()
        if nph >= 3:
            _build_p3a(nc, S, es0, aT, wo, xo, x1d, cons, rn, nrtmp, ps)
            S.barrier()
        if nph >= 4:
            _build_p3b(nc, S, es0, x1d, x1qT, r448, idt, ps)
            S.barrier()
        if nph >= 5:
            _build_p4(nc, S, es0, x1qT, gw, uw, dw, gqT, uqT, dqT, x1d, out_d,
                      cons, rn, r448, nrtmp, idt, ps)
        if os.environ.get("KPRINT"):
            print("SEM TOTALS:", {k: c.v for k, c in S.cnt.items()})
        for sem in STORE_SEMS:
            S.wait("sync", sem, S.cnt[sem].v)

        with nc.Block() as block:
            @block.sync
            def _(e):
                for f in S.ev["sync"]:
                    f(e)

            @block.tensor
            def _(e):
                for f in S.ev["tensor"]:
                    f(e)

            @block.vector
            def _(e):
                for f in S.ev["vector"]:
                    f(e)

            @block.scalar
            def _(e):
                for f in S.ev["scalar"]:
                    f(e)

            @block.gpsimd
            def _(e):
                for f in S.ev["gpsimd"]:
                    f(e)

    _NC_CACHE["nc"] = nc
    return nc



def _build_p1(nc, S, es0, xT, xoT, wqkv, KT, QT, Vd, ps):
    import contextlib
    with contextlib.ExitStack() as es:
        TC = 1024
        xTs = es.enter_context(nc.sbuf_tensor("p1xT", [128, 16 * TC], F32))
        xoTs = es.enter_context(nc.sbuf_tensor("p1xo", [128, 16 * 512], F32))
        wc = [es.enter_context(nc.sbuf_tensor(f"p1w{j}", [128, 16 * 128], F32))
              for j in range(2)]
        kst = [es.enter_context(nc.sbuf_tensor(f"p1k{j}", [128, 512], F32))
               for j in range(4)]
        bank_mark = [0] * 8
        stage_mark = [0] * 4
        chunk_pe_mark = 0
        wc_mark = [0, 0]
        ogi = 0
        stg = 0

        def psgroup(bank, emit_mms, evac_out_ap, evac_w, store_out):
            nonlocal stg
            S.wait("tensor", "aa", bank_mark[bank])
            emit_mms(ps[bank][:, 0:evac_w])
            pem = S.pub("tensor", "pe")
            slot = stg % 4
            stg += 1
            S.wait("scalar", f"st{slot}", stage_mark[slot])
            S.wait("scalar", "pe", pem)
            S.op("scalar", "activation", out=kst[slot][:, 0:evac_w],
                 in_=ps[bank][:, 0:evac_w], func=ACTF.Copy)
            aam = S.pub("scalar", "aa")
            bank_mark[bank] = aam
            S.wait("gpsimd", "aa", aam)
            g = S.dma("gpsimd", store_out, kst[slot][:, 0:evac_w], f"st{slot}")
            stage_mark[slot] = g
            return pem

        for tc in range(4):
            S.wait("sync", "pe", chunk_pe_mark)
            S.dfence("sync", "sldA")
            tx = 0
            for ft in range(16):
                tx = S.dma("sync", xTs[:, ft * TC:(ft + 1) * TC],
                           xT[ft * 128:(ft + 1) * 128, tc * TC:(tc + 1) * TC],
                           "sldA")
            for ft in range(16):
                tx = S.dma("sync", xoTs[:, ft * 512:(ft + 1) * 512],
                           xoT[ft * 128:(ft + 1) * 128, tc * 512:(tc + 1) * 512],
                           "sldA")
            xtgt = tx
            last_pe = 0
            for kind, h in ([("k", h) for h in range(H)]
                            + [("q", h) for h in range(H)]
                            + [("v", h) for h in range(H)]):
                j = ogi % 2
                ogi += 1
                wsem = "wA" if j == 0 else "wB"
                c0 = {"q": 0, "k": D, "v": 2 * D}[kind] + h * 128
                S.wait("sync", "pe", wc_mark[j])
                S.dfence("sync", wsem)
                twc = 0
                for ft in range(16):
                    twc = S.dma("sync", wc[j][:, ft * 128:(ft + 1) * 128],
                                wqkv[ft * 128:(ft + 1) * 128, c0:c0 + 128],
                                wsem)
                S.wait("tensor", "sldA", xtgt)
                S.wait("tensor", wsem, twc)
                if kind == "k":
                    for nck in range(2):
                        def mk(pso, j=j, nck=nck):
                            for ft in range(16):
                                S.op("tensor", "matmul", pso,
                                     wc[j][:, ft * 128:(ft + 1) * 128],
                                     xTs[:, ft * TC + nck * 512:
                                         ft * TC + nck * 512 + 512],
                                     start=(ft == 0), stop=(ft == 15))
                        last_pe = psgroup((ogi + nck) % 4, mk, None, 512,
                                          KT[h * 128:(h + 1) * 128,
                                             tc * TC + nck * 512:
                                             tc * TC + nck * 512 + 512])
                elif kind == "q":
                    def mk(pso, j=j):
                        for ft in range(16):
                            S.op("tensor", "matmul", pso,
                                 wc[j][:, ft * 128:(ft + 1) * 128],
                                 xoTs[:, ft * 512:(ft + 1) * 512],
                                 start=(ft == 0), stop=(ft == 15))
                    last_pe = psgroup(ogi % 4, mk, None, 512,
                                      QT[h * 128:(h + 1) * 128,
                                         tc * 512:(tc + 1) * 512])
                else:
                    for tt in range(8):
                        def mk(pso, j=j, tt=tt):
                            for ft in range(16):
                                S.op("tensor", "matmul", pso,
                                     xTs[:, ft * TC + tt * 128:
                                         ft * TC + tt * 128 + 128],
                                     wc[j][:, ft * 128:(ft + 1) * 128],
                                     start=(ft == 0), stop=(ft == 15))
                        last_pe = psgroup((ogi + tt) % 4, mk, None, 128,
                                          Vd[h, tc * TC + tt * 128:
                                             tc * TC + tt * 128 + 128, :])
                wc_mark[j] = last_pe
            chunk_pe_mark = last_pe



def _build_p2(nc, S, es0, KT, QT, Vd, aT, maskt, idt, stats, ps):
    import contextlib
    with contextlib.ExitStack() as es:
        KTh = [es.enter_context(nc.sbuf_tensor(f"p2K{j}", [128, T], F32))
               for j in range(2)]
        Vh = [es.enter_context(nc.sbuf_tensor(f"p2V{j}", [128, T], F32))
              for j in range(2)]
        QTh = [es.enter_context(nc.sbuf_tensor(f"p2Q{j}", [128, OWN], F32))
               for j in range(2)]
        sb_ = [es.enter_context(nc.sbuf_tensor(f"p2S{j}", [128, T], F32))
               for j in range(2)]
        pb_ = [es.enter_context(nc.sbuf_tensor(f"p2P{j}", [128, T], F32))
               for j in range(2)]
        sd_ = [es.enter_context(nc.sbuf_tensor(f"p2sd{j}", [128, 256], F32))
               for j in range(2)]
        pd_ = [es.enter_context(nc.sbuf_tensor(f"p2pd{j}", [128, 256], F32))
               for j in range(2)]
        pts = es.enter_context(nc.sbuf_tensor("p2pt", [128, 4 * 128], F32))
        ots = [es.enter_context(nc.sbuf_tensor(f"p2o{j}", [128, 128], F32))
               for j in range(2)]
        head_pe_mark = [0, 0]
        load_tgt = [0] * H
        sbank_mark = [0, 0]
        pt_slot_aa = [0] * 4
        pt_slot_pe = [0] * 4
        psO_mark = [0, 0]
        ots_mark = [0, 0]
        dve_done = [0, 0]
        ptr_done = [0, 0]

        def stc(tp, col):
            return stats[:, tp * 8 + col: tp * 8 + col + 1]

        import os
        nh = int(os.environ.get("KP2H", str(H)))
        for h in range(nh):
            hp = h % 2
            sB = f"sB{hp}"
            S.wait("sync", "pe", head_pe_mark[hp])
            S.dfence("sync", sB)
            S.dma("sync", KTh[hp][:], KT[h * 128:(h + 1) * 128, :], sB)
            S.dma("sync", QTh[hp][:], QT[h * 128:(h + 1) * 128, :], sB)
            t = 0
            for g in range(32):
                t = S.dma("sync",
                          Vh[hp][:, g * 128:(g + 1) * 128],
                          Vd[h, g * 128:(g + 1) * 128, :], sB)
            load_tgt[h] = t
            S.wait("tensor", sB, load_tgt[h])
            last_pe = 0
            for i in range(int(os.environ.get("KP2T", str(NTILE)))):
                tp = i % 2
                sb = sb_[tp]
                pb = pb_[tp]
                sd = sd_[tp]
                pd = pd_[tp]
                nf = 2 * i * 128
                q0 = i * 128
                S.wait("scalar", "vv", dve_done[tp])
                S.wait("scalar", "pe", ptr_done[tp])
                sc = 0
                evs = []
                for o in range(0, nf, 512):
                    w = min(512, nf - o)
                    bk = 6 + (sc % 2)
                    sc += 1
                    S.wait("tensor", "aa", sbank_mark[bk - 6])
                    S.op("tensor", "matmul", ps[bk][:, 0:w],
                         QTh[hp][:, q0:q0 + 128], KTh[hp][:, o:o + w],
                         start=True, stop=True)
                    pem = S.pub("tensor", "pe")
                    S.wait("scalar", "pe", pem)
                    S.op("scalar", "activation", out=sb[:, o:o + w],
                         in_=ps[bk][:, 0:w], func=ACTF.Copy)
                    aam = S.pub("scalar", "aa")
                    sbank_mark[bk - 6] = aam
                    evs.append(aam)
                bk = 6 + (sc % 2)
                S.wait("tensor", "aa", sbank_mark[bk - 6])
                S.op("tensor", "matmul", ps[bk][:, 0:256],
                     QTh[hp][:, q0:q0 + 128], KTh[hp][:, nf:nf + 256],
                     start=True, stop=True)
                pem = S.pub("tensor", "pe")
                S.wait("scalar", "pe", pem)
                S.op("scalar", "activation", out=sd[:],
                     in_=ps[bk][:, 0:256], func=ACTF.Copy)
                aam = S.pub("scalar", "aa")
                sbank_mark[bk - 6] = aam
                evs.append(aam)
                S.wait("vector", "aa", evs[-1])
                if nf > 0:
                    S.op("vector", "tensor_reduce", stc(tp, 0), sb[:, 0:nf],
                         axis=AX.X, op=ALU.max)
                S.op("vector", "tensor_reduce", stc(tp, 1), sd[:],
                     axis=AX.X, op=ALU.max)
                S.op("vector", "drain")
                if nf > 0:
                    S.op("vector", "tensor_tensor", stc(tp, 1), stc(tp, 0),
                         stc(tp, 1), op=ALU.max)
                    S.op("vector", "drain")
                S.op("vector", "tensor_scalar", stc(tp, 2), stc(tp, 1),
                     -SCALE, None, op0=ALU.mult)
                vvm = S.pub("vector", "vv")
                S.wait("scalar", "vv", vvm)
                if nf > 0:
                    S.op("scalar", "activation", out=pb[:, 0:nf],
                         in_=sb[:, 0:nf], func=ACTF.Exp, bias=stc(tp, 2),
                         scale=SCALE, accum_out=stc(tp, 3))
                S.op("scalar", "activation", out=pd[:], in_=sd[:],
                     func=ACTF.Exp, bias=stc(tp, 2), scale=SCALE)
                aam = S.pub("scalar", "aa")
                S.wait("vector", "aa", aam)
                S.op("vector", "tensor_tensor", pd[:], pd[:], maskt[:],
                     op=ALU.mult)
                S.op("vector", "drain")
                S.op("vector", "tensor_reduce", stc(tp, 4), pd[:],
                     axis=AX.X, op=ALU.add)
                S.op("vector", "drain")
                if nf > 0:
                    S.op("vector", "tensor_tensor", stc(tp, 5), stc(tp, 3),
                         stc(tp, 4), op=ALU.add)
                else:
                    S.op("vector", "tensor_scalar", stc(tp, 5), stc(tp, 4),
                         0.0, None, op0=ALU.add)
                vvm = S.pub("vector", "vv")
                S.wait("scalar", "vv", vvm)
                S.op("scalar", "activation", out=stc(tp, 6), in_=stc(tp, 5),
                     func=ACTF.Ln)
                aam = S.pub("scalar", "aa")
                S.wait("vector", "aa", aam)
                S.op("vector", "tensor_tensor", stc(tp, 7), stc(tp, 2),
                     stc(tp, 6), op=ALU.subtract)
                vvm = S.pub("vector", "vv")
                S.wait("scalar", "vv", vvm)
                if nf > 0:
                    S.op("scalar", "activation", out=pb[:, 0:nf],
                         in_=sb[:, 0:nf], func=ACTF.Exp, bias=stc(tp, 7),
                         scale=SCALE)
                S.op("scalar", "activation", out=pd[:], in_=sd[:],
                     func=ACTF.Exp, bias=stc(tp, 7), scale=SCALE)
                aam = S.pub("scalar", "aa")
                S.wait("vector", "aa", aam)
                S.op("vector", "tensor_tensor", pd[:], pd[:], maskt[:],
                     op=ALU.mult)
                vvm = S.pub("vector", "vv")
                dve_done[tp] = vvm
                nkt = 2 * i + 2
                S.wait("tensor", "vv", vvm)
                S.wait("tensor", "aa", psO_mark[tp])

                def src_ap(kt):
                    if kt < 2 * i:
                        return pb[:, kt * 128:(kt + 1) * 128]
                    return pd[:, (kt - 2 * i) * 128:(kt - 2 * i + 1) * 128]

                pslots = (ps[0], ps[1], ps[3], ps[2])
                pvm = [0] * nkt
                for kt in range(nkt + 1):
                    if kt < nkt:
                        slot = kt % 4
                        S.wait("tensor", "aa", pt_slot_aa[slot])
                        S.op("tensor", "transpose",
                             pslots[slot][:, 0:128],
                             src_ap(kt), idt[:])
                        tm = S.pub("tensor", "pe")
                        ptr_done[tp] = tm
                        S.wait("scalar", "pe", tm)
                        if kt >= 4:
                            S.wait("scalar", "pe", pvm[kt - 4])
                        S.op("scalar", "activation",
                             out=pts[:, slot * 128:(slot + 1) * 128],
                             in_=pslots[slot][:, 0:128],
                             func=ACTF.Copy)
                        pt_slot_aa[slot] = S.pub("scalar", "aa")
                    if kt >= 1:
                        k2 = kt - 1
                        slot2 = k2 % 4
                        S.wait("tensor", "aa", pt_slot_aa[slot2])
                        S.op("tensor", "matmul", ps[4 + tp][:, 0:128],
                             Vh[hp][:, k2 * 128:(k2 + 1) * 128],
                             pts[:, slot2 * 128:(slot2 + 1) * 128],
                             start=(k2 == 0), stop=(k2 == nkt - 1))
                        pvm[k2] = S.pub("tensor", "pe")
                        pt_slot_pe[slot2] = pvm[k2]
                last_pe = pvm[nkt - 1]
                S.wait("scalar", "pe", last_pe)
                S.wait("scalar", f"ost{tp}", ots_mark[tp])
                S.op("scalar", "activation", out=ots[tp][:],
                     in_=ps[4 + tp][:, 0:128], func=ACTF.Copy)
                aam = S.pub("scalar", "aa")
                psO_mark[tp] = aam
                S.wait("gpsimd", "aa", aam)
                g = S.dma("gpsimd", aT[h * 128:(h + 1) * 128,
                                       i * 128:(i + 1) * 128],
                          ots[tp][:], f"ost{tp}")
                ots_mark[tp] = g
            head_pe_mark[hp] = last_pe


def emit_ln(S, x_ap, out_ap, stats, ceps, sqscr, col0=16):
    V = EngProxy(S, "vector")
    mt = stats[:, col0:col0 + 1]
    sqt = stats[:, col0 + 1:col0 + 2]
    vart = stats[:, col0 + 2:col0 + 3]
    rstd = stats[:, col0 + 3:col0 + 4]
    nbt = stats[:, col0 + 4:col0 + 5]
    tmp = stats[:, col0 + 5:col0 + 6]
    S.op("vector", "tensor_reduce", mt, x_ap, axis=AX.X, op=ALU.add)
    S.op("vector", "drain")
    S.op("vector", "tensor_scalar", mt, mt, 2.0 ** -11, None, op0=ALU.mult)
    vvm = S.pub("vector", "vv")
    S.wait("scalar", "vv", vvm)
    S.op("scalar", "activation", out=sqscr, in_=x_ap, func=ACTF.Square,
         accum_out=sqt)
    aam = S.pub("scalar", "aa")
    S.wait("vector", "aa", aam)
    S.op("vector", "tensor_scalar", vart, sqt, 2.0 ** -11, None, op0=ALU.mult)
    S.op("vector", "tensor_tensor", tmp, mt, mt, op=ALU.mult)
    S.op("vector", "drain")
    S.op("vector", "tensor_tensor", vart, vart, tmp, op=ALU.subtract)
    vvm = S.pub("vector", "vv")
    S.wait("scalar", "vv", vvm)
    S.op("scalar", "activation", out=vart, in_=vart, func=ACTF.Sqrt, bias=ceps)
    aam = S.pub("scalar", "aa")
    S.wait("vector", "aa", aam)
    emit_nr_recip(V, rstd, vart, tmp)
    S.op("vector", "tensor_tensor", nbt, mt, rstd, op=ALU.mult)
    S.op("vector", "drain")
    S.op("vector", "tensor_scalar", nbt, nbt, -1.0, None, op0=ALU.mult)
    vvm = S.pub("vector", "vv")
    S.wait("scalar", "vv", vvm)
    S.op("scalar", "activation", out=out_ap, in_=x_ap, func=ACTF.Identity,
         scale=rstd, bias=nbt)
    return S.pub("scalar", "aa")



def _build_p3a(nc, S, es0, aT, wo, xo, x1d, cons, rn, nrtmp, ps):
    import contextlib
    with contextlib.ExitStack() as es:
        aTs = es.enter_context(nc.sbuf_tensor("p3a", [128, 16 * 512], F32))
        wos = es.enter_context(nc.sbuf_tensor("p3w", [128, 16 * 512], F32))
        xos = es.enter_context(nc.sbuf_tensor("p3x", [128, 4 * D], F32))
        x1a = es.enter_context(nc.sbuf_tensor("p3y", [128, 4 * D], F32))
        sqs = es.enter_context(nc.sbuf_tensor("p3q", [128, D], F32))
        x1s = es.enter_context(nc.sbuf_tensor("p3z", [128, D], F32))
        lns = es.enter_context(nc.sbuf_tensor("p3s", [128, 8], F32))
        ceps = cons[:, 1:2]
        bank_mark = [0] * 8
        pe_chunk_mark = 0
        x1s_mark = 0
        xos_vv = 0
        ln_aa = 0
        for qc in range(4):
            S.wait("sync", "pe", pe_chunk_mark)
            S.wait("sync", "vv", xos_vv)
            S.dfence("sync", "p3L")
            ta = 0
            for dt in range(16):
                ta = S.dma("sync", aTs[:, dt * 512:(dt + 1) * 512],
                           aT[dt * 128:(dt + 1) * 128,
                              qc * 512:(qc + 1) * 512], "p3L")
            for q4 in range(4):
                ta = S.dma("sync", xos[:, q4 * D:(q4 + 1) * D],
                           xo[qc * 512 + q4 * 128: qc * 512 + q4 * 128 + 128,
                              :], "p3L")
            last_pe = 0
            for dc in range(4):
                S.wait("sync", "pe", last_pe if dc > 0 else pe_chunk_mark)
                S.dfence("sync", "p3W")
                tw = 0
                for dt in range(16):
                    tw = S.dma("sync", wos[:, dt * 512:(dt + 1) * 512],
                               wo[dt * 128:(dt + 1) * 128,
                                  dc * 512:(dc + 1) * 512], "p3W")
                S.wait("tensor", "p3L", ta)
                S.wait("tensor", "p3W", tw)
                for q4 in range(4):
                    bk = (dc * 4 + q4) % 6
                    S.wait("tensor", "vv", bank_mark[bk])
                    for dt in range(16):
                        S.op("tensor", "matmul", ps[bk][:, 0:512],
                             aTs[:, dt * 512 + q4 * 128:
                                 dt * 512 + q4 * 128 + 128],
                             wos[:, dt * 512:(dt + 1) * 512],
                             start=(dt == 0), stop=(dt == 15))
                    pem = S.pub("tensor", "pe")
                    last_pe = pem
                    S.wait("vector", "pe", pem)
                    if dc == 0 and q4 == 0:
                        S.wait("vector", "aa", ln_aa)
                    S.op("vector", "tensor_tensor",
                         x1a[:, q4 * D + dc * 512: q4 * D + dc * 512 + 512],
                         ps[bk][:, 0:512],
                         xos[:, q4 * D + dc * 512: q4 * D + dc * 512 + 512],
                         op=ALU.add)
                    bank_mark[bk] = S.pub("vector", "vv")
                    xos_vv = bank_mark[bk]
            pe_chunk_mark = last_pe
            for q4 in range(4):
                S.wait("vector", "vv", bank_mark[(3 * 4 + q4) % 6])
                S.wait("scalar", "p3S", x1s_mark)
                aam = emit_ln(S, x1a[:, q4 * D:(q4 + 1) * D], x1s[:],
                              lns, ceps, sqs[:], col0=0)
                ln_aa = aam
                S.wait("gpsimd", "aa", aam)
                x1s_mark = S.dma("gpsimd",
                                 x1d[qc * 512 + q4 * 128:
                                     qc * 512 + q4 * 128 + 128, :],
                                 x1s[:], "p3S")



def _build_p3b(nc, S, es0, x1d, x1qT, r448, idt, ps):
    import contextlib
    with contextlib.ExitStack() as es:
        x1t = es.enter_context(nc.sbuf_tensor("p3bx", [128, 2 * D], F32))
        scr = quant_scratch(nc, es, "p3bq", D, 2)
        tst = es.enter_context(nc.sbuf_tensor("p3bt", [128, 4 * 128], F32))
        V = EngProxy(S, "vector")
        slot_aa = [0] * 4
        slot_gst = [0] * 4
        pair_pe = 0
        for pr in range(8):
            S.wait("sync", "pe", pair_pe)
            S.dfence("sync", "p3bL")
            tl = 0
            for j in range(2):
                tl = S.dma("sync", x1t[:, j * D:(j + 1) * D],
                           x1d[(pr * 2 + j) * 128:(pr * 2 + j + 1) * 128, :],
                           "p3bL")
            S.wait("vector", "p3bL", tl)
            tiles = [(x1t[:, j * D:(j + 1) * D], x1t[:, j * D:(j + 1) * D])
                     for j in range(2)]
            emit_quant(V, tiles, scr, r448[:])
            vvm = S.pub("vector", "vv")
            S.wait("tensor", "vv", vvm)
            for j in range(2):
                for kb in range(16):
                    slot = (j * 16 + kb) % 4
                    S.wait("tensor", "aa", slot_aa[slot])
                    S.op("tensor", "transpose",
                         ps[slot][:, 0:128],
                         x1t[:, j * D + kb * 128: j * D + kb * 128 + 128],
                         idt[:])
                    tm = S.pub("tensor", "pe")
                    pair_pe = tm
                    S.wait("scalar", "pe", tm)
                    S.wait("scalar", f"ts{slot}", slot_gst[slot])
                    S.op("scalar", "activation",
                         out=tst[:, slot * 128:(slot + 1) * 128],
                         in_=ps[slot][:, 0:128],
                         func=ACTF.Copy)
                    slot_aa[slot] = S.pub("scalar", "aa")
                    S.wait("gpsimd", "aa", slot_aa[slot])
                    slot_gst[slot] = S.dma(
                        "gpsimd",
                        x1qT[kb * 128:(kb + 1) * 128,
                             (pr * 2 + j) * 128:(pr * 2 + j + 1) * 128],
                        tst[:, slot * 128:(slot + 1) * 128], f"ts{slot}")



def _build_p4(nc, S, es0, x1qT, gw, uw, dw, gqT, uqT, dqT, x1d, out_d,
              cons, rn, r448, nrtmp, idt, ps):
    import contextlib
    with contextlib.ExitStack() as es:
        xq = es.enter_context(nc.sbuf_tensor("p4x", [128, 16 * 512], F32))
        outa = es.enter_context(nc.sbuf_tensor("p4o", [128, 4 * D], F32))
        wsl = es.enter_context(nc.sbuf_tensor("p4w", [128, 4 * D], F32))
        wqt = es.enter_context(nc.sbuf_tensor("p4t", [128, 8192], F32))
        hs = es.enter_context(nc.sbuf_tensor("p4h", [128, 4 * 512], F32))
        hb = es.enter_context(nc.sbuf_tensor("p4b", [128, 4 * 512], F32))
        hqt = es.enter_context(nc.sbuf_tensor("p4ht", [128, 4 * 512], F32))
        scr = quant_scratch(nc, es, "p4q", 1024, 2)
        x1t = es.enter_context(nc.sbuf_tensor("p4x1", [128, D], F32))
        osg = es.enter_context(nc.sbuf_tensor("p4os", [128, D], F32))
        lns = es.enter_context(nc.sbuf_tensor("p4ln", [128, 8], F32))
        ceps = cons[:, 1:2]
        V = EngProxy(S, "vector")
        bank_mark = [0] * 8
        bank_sem = ["aa"] * 8
        slot_aa = [0] * 4
        silu_mark = [0] * 4
        pe_slab_mark = 0
        pe_q_mark = 0
        out_gst = 0
        last_ln_aa = 0

        def transpose_block(src_ap, dst_ap, slot):
            slot = slot % 2
            pb_ = ps[2] if slot == 0 else ps[7]
            S.wait("tensor", "aa", slot_aa[slot])
            S.op("tensor", "transpose", pb_[:, 0:128], src_ap, idt[:])
            tm = S.pub("tensor", "pe")
            S.wait("scalar", "pe", tm)
            S.op("scalar", "activation", out=dst_ap,
                 in_=pb_[:, 0:128], func=ACTF.Copy)
            slot_aa[slot] = S.pub("scalar", "aa")
            return tm, slot_aa[slot]

        import os
        ncs = int(os.environ.get("KP4C", "16"))
        for rq in range(int(os.environ.get("KP4R", "4"))):
            S.wait("sync", "pe", pe_q_mark)
            S.dfence("sync", "p4X")
            tl = 0
            for kb in range(16):
                tl = S.dma("sync", xq[:, kb * 512:(kb + 1) * 512],
                           x1qT[kb * 128:(kb + 1) * 128,
                                rq * 512:(rq + 1) * 512], "p4X")
            xq_tgt = tl
            last_pe = 0
            for cs in range(ncs):
                for wi, (w_in, wqT_d) in enumerate(((gw, gqT), (uw, uqT))):
                    S.wait("sync", "pe", last_pe if (cs or wi or rq) else 0)
                    tw = 0
                    if rq == 0:
                        S.dfence("sync", "p4W")
                        for ct in range(4):
                            tw = S.dma("sync", wsl[:, ct * D:(ct + 1) * D],
                                       w_in[cs * 512 + ct * 128:
                                            cs * 512 + ct * 128 + 128, :],
                                       "p4W")
                        S.wait("vector", "p4W", tw)
                        for cl in range(4):
                            tiles = [(wsl[:, (2 * cl + j) * 1024:
                                          (2 * cl + j + 1) * 1024],) * 2
                                     for j in range(2)]
                            emit_quant(V, tiles, scr, r448[:])
                        vvm = S.pub("vector", "vv")
                        S.wait("tensor", "vv", vvm)
                        S.wait("scalar", "p4S", S.cnt["p4S"].v)
                        for ct in range(4):
                            for kb in range(16):
                                tm, am = transpose_block(
                                    wsl[:, ct * D + kb * 128:
                                        ct * D + kb * 128 + 128],
                                    wqt[:, kb * 512 + ct * 128:
                                        kb * 512 + ct * 128 + 128],
                                    (ct * 16 + kb) % 4)
                        wq_aa = am
                        S.wait("gpsimd", "aa", wq_aa)
                        for kb in range(16):
                            S.dma("gpsimd",
                                  wqT_d[kb * 128:(kb + 1) * 128,
                                        cs * 512:(cs + 1) * 512],
                                  wqt[:, kb * 512:(kb + 1) * 512], "p4S")
                    else:
                        S.dfence("sync", "p4S")
                        S.dfence("sync", "p4Q")
                        for kb in range(16):
                            tw = S.dma("sync", wqt[:, kb * 512:(kb + 1) * 512],
                                       wqT_d[kb * 128:(kb + 1) * 128,
                                             cs * 512:(cs + 1) * 512], "p4Q")
                        wq_aa = None
                        S.wait("tensor", "p4Q", tw)
                    if cs == 0 and wi == 0:
                        S.wait("tensor", "p4X", xq_tgt)
                    if wq_aa is not None:
                        S.wait("tensor", "aa", wq_aa)
                    for rt in range(4):
                        bk = rt % 2 if wi == 0 else 3 + rt % 2
                        S.wait("tensor", bank_sem[bk], bank_mark[bk])
                        if wi == 1:
                            S.wait("vector", "aa", silu_mark[rt])
                        for kb in range(16):
                            S.op("tensor", "matmul", ps[bk][:, 0:512],
                                 xq[:, kb * 512 + rt * 128:
                                     kb * 512 + rt * 128 + 128],
                                 wqt[:, kb * 512:(kb + 1) * 512],
                                 start=(kb == 0), stop=(kb == 15))
                        pem = S.pub("tensor", "pe")
                        last_pe = pem
                        if wi == 0:
                            S.wait("scalar", "pe", pem)
                            S.op("scalar", "activation",
                                 out=hs[:, rt * 512:(rt + 1) * 512],
                                 in_=ps[bk][:, 0:512], func=ACTF.Silu)
                            bank_mark[bk] = S.pub("scalar", "aa")
                            silu_mark[rt] = bank_mark[bk]
                            bank_sem[bk] = "aa"
                        else:
                            S.wait("vector", "pe", pem)
                            S.op("vector", "tensor_tensor",
                                 hb[:, rt * 512:(rt + 1) * 512],
                                 ps[bk][:, 0:512],
                                 hs[:, rt * 512:(rt + 1) * 512], op=ALU.mult)
                            bank_mark[bk] = S.pub("vector", "vv")
                            bank_sem[bk] = "vv"
                for cl in range(2):
                    tiles = [(hb[:, (2 * cl + j) * 512:
                                  (2 * cl + j + 1) * 512],) * 2
                             for j in range(2)]
                    emit_quant(V, tiles, scr, r448[:])
                vvm = S.pub("vector", "vv")
                S.wait("tensor", "vv", vvm)
                for rt in range(4):
                    for cb in range(4):
                        tm, am = transpose_block(
                            hb[:, rt * 512 + cb * 128: rt * 512 + cb * 128 + 128],
                            hqt[:, cb * 512 + rt * 128: cb * 512 + rt * 128 + 128],
                            (rt * 4 + cb) % 4)
                hq_aa = am
                S.wait("sync", "pe", last_pe)
                td = 0
                if rq == 0:
                    S.dfence("sync", "p4W")
                    for dt in range(16):
                        td = S.dma("sync", wsl[:, dt * 512:(dt + 1) * 512],
                                   dw[dt * 128:(dt + 1) * 128,
                                      cs * 512:(cs + 1) * 512], "p4W")
                    S.wait("vector", "p4W", td)
                    for cl in range(4):
                        for half in range(2):
                            tiles = []
                            for j in range(2):
                                ap = wsl[:, (4 * cl + 2 * j + half) * 512:
                                         (4 * cl + 2 * j + half + 1) * 512]
                                tiles.append((ap, ap))
                            emit_quant(V, tiles, scr, r448[:])
                    vvm = S.pub("vector", "vv")
                    S.wait("tensor", "vv", vvm)
                    S.wait("scalar", "p4S", S.cnt["p4S"].v)
                    for dt in range(16):
                        for cb in range(4):
                            tm, am = transpose_block(
                                wsl[:, dt * 512 + cb * 128:
                                    dt * 512 + cb * 128 + 128],
                                wqt[:, cb * 2048 + dt * 128:
                                    cb * 2048 + dt * 128 + 128],
                                (dt * 4 + cb) % 4)
                    dq_aa = am
                    S.wait("gpsimd", "aa", dq_aa)
                    for cb in range(4):
                        S.dma("gpsimd",
                              dqT[cs * 512 + cb * 128: cs * 512 + cb * 128 + 128,
                                  :],
                              wqt[:, cb * 2048:(cb + 1) * 2048], "p4S")
                else:
                    S.dfence("sync", "p4S")
                    S.dfence("sync", "p4Q")
                    for cb in range(4):
                        td = S.dma("sync", wqt[:, cb * 2048:(cb + 1) * 2048],
                                   dqT[cs * 512 + cb * 128:
                                       cs * 512 + cb * 128 + 128, :], "p4Q")
                    S.wait("tensor", "p4Q", td)
                S.wait("tensor", "aa", dq_aa if rq == 0 else hq_aa)
                for rt in range(4):
                    for dc2 in range(4):
                        bk = 5 + (rt * 4 + dc2) % 2
                        S.wait("tensor", bank_sem[bk], bank_mark[bk])
                        for ct in range(4):
                            S.op("tensor", "matmul", ps[bk][:, 0:512],
                                 hqt[:, ct * 512 + rt * 128:
                                     ct * 512 + rt * 128 + 128],
                                 wqt[:, ct * 2048 + dc2 * 512:
                                     ct * 2048 + dc2 * 512 + 512],
                                 start=(ct == 0), stop=(ct == 3))
                        pem = S.pub("tensor", "pe")
                        last_pe = pem
                        S.wait("vector", "pe", pem)
                        if cs == 0:
                            S.op("vector", "tensor_copy",
                                 outa[:, rt * D + dc2 * 512:
                                      rt * D + dc2 * 512 + 512],
                                 ps[bk][:, 0:512])
                        else:
                            S.op("vector", "tensor_tensor",
                                 outa[:, rt * D + dc2 * 512:
                                      rt * D + dc2 * 512 + 512],
                                 ps[bk][:, 0:512],
                                 outa[:, rt * D + dc2 * 512:
                                      rt * D + dc2 * 512 + 512], op=ALU.add)
                        bank_mark[bk] = S.pub("vector", "vv")
                        bank_sem[bk] = "vv"
            pe_q_mark = last_pe
            for rt in range(4):
                S.wait("sync", "aa", last_ln_aa)
                S.dfence("sync", "p4R")
                tl = S.dma("sync", x1t[:],
                           x1d[rq * 512 + rt * 128: rq * 512 + rt * 128 + 128,
                               :], "p4R")
                S.wait("vector", "p4R", tl)
                S.wait("vector", "vv", bank_mark[5])
                S.wait("vector", "vv", bank_mark[6])
                S.op("vector", "tensor_tensor", x1t[:], x1t[:],
                     outa[:, rt * D:(rt + 1) * D], op=ALU.add)
                S.pub("vector", "vv")
                S.wait("scalar", "p4O", out_gst)
                aam = emit_ln(S, x1t[:], osg[:], lns, ceps, osg[:], col0=0)
                last_ln_aa = aam
                S.wait("gpsimd", "aa", aam)
                out_gst = S.dma("gpsimd",
                                out_d[rq * 512 + rt * 128:
                                      rq * 512 + rt * 128 + 128, :],
                                osg[:], "p4O")




def kernel(L0_attn_x, L0_attn_w_qkv, L0_attn_w_o, L0_ffn_gate_w, L0_ffn_up_w,
           L0_ffn_down_w):
    x = np.ascontiguousarray(np.asarray(L0_attn_x, dtype=np.float32))
    wqkv = np.ascontiguousarray(np.asarray(L0_attn_w_qkv, dtype=np.float32))
    wo = np.ascontiguousarray(np.asarray(L0_attn_w_o, dtype=np.float32))
    gwv = np.ascontiguousarray(np.asarray(L0_ffn_gate_w, dtype=np.float32))
    uwv = np.ascontiguousarray(np.asarray(L0_ffn_up_w, dtype=np.float32))
    dwv = np.ascontiguousarray(np.asarray(L0_ffn_down_w, dtype=np.float32))

    nc = build()
    ident = np.eye(128, dtype=np.float32)
    consts = np.zeros((128, 4), np.float32)
    consts[:, 0] = 448.0
    consts[:, 1] = 1e-5
    consts[:, 2] = 2048.0
    lt = np.tril(np.ones((128, 128), np.float32))
    ones = np.ones((128, 128), np.float32)
    zeros = np.zeros((128, 128), np.float32)
    in_maps = []
    for c in range(NCORES):
        b, p = c // 2, c % 2
        xb = x[b]
        xov = np.ascontiguousarray(
            xb.reshape(T // 128, 128, D)[p::2].reshape(OWN, D))
        m01 = np.concatenate([lt, zeros] if p == 0 else [ones, lt],
                             axis=1)
        in_maps.append(dict(
            xT=np.ascontiguousarray(xb.T),
            xoT=np.ascontiguousarray(xov.T),
            xo=xov,
            wqkv=wqkv, wo=wo, gw=gwv, uw=uwv, dw=dwv,
            m01=np.ascontiguousarray(m01),
            ident=ident, consts=consts,
        ))
    res = run_bass_kernel_spmd(nc, in_maps, core_ids=list(range(NCORES)))
    out = np.empty((B, T, D), np.float32)
    for c in range(NCORES):
        b, p = c // 2, c % 2
        out[b].reshape(T // 128, 128, D)[p::2] = \
            res.results[c]["out"].reshape(NTILE, 128, D)
    return out
